# version 27
# speedup vs baseline: 1.2574x; 1.0028x over previous
import contextlib

import numpy as np
import ml_dtypes

import concourse.bass as bass
import concourse.mybir as mybir
from concourse.bass_utils import run_bass_kernel_spmd

F32 = mybir.dt.float32
BF16 = mybir.dt.bfloat16
I32 = mybir.dt.int32
AF = mybir.ActivationFunctionType
ALU = mybir.AluOpType

B, S, D = 4, 4096, 1024
NH, HD = 16, 64
HF = 3072
SQ = 2048
NQT = 16
DC = 8
EPS = 1e-5
QSCALE = 0.125
NEG = -1.0e38
NSLAB = 6
SLAB = HF // NSLAB
NSLAB2 = 12
SLAB2 = HF // NSLAB2
CW = 256


class Sem:
    def __init__(self, es, nc, name):
        self.h = es.enter_context(nc.semaphore(name))
        self.n = 0

    def inc(self, k=1):
        self.n += k
        return self.n


class EngBufs:
    NAMES = ("sync", "gpsimd", "scalar", "vector", "tensor")

    def __init__(self):
        self.b = {n: [] for n in self.NAMES}

    def on(self, name, fn):
        self.b[name].append(fn)

    def flush(self, nc, block_name):
        with nc.Block(block_name) as blk:
            if self.b["sync"]:
                blk.sync(lambda e, ops=self.b["sync"]: [f(e) for f in ops])
            if self.b["gpsimd"]:
                blk.gpsimd(lambda e, ops=self.b["gpsimd"]: [f(e) for f in ops])
            if self.b["scalar"]:
                blk.scalar(lambda e, ops=self.b["scalar"]: [f(e) for f in ops])
            if self.b["vector"]:
                blk.vector(lambda e, ops=self.b["vector"]: [f(e) for f in ops])
            if self.b["tensor"]:
                blk.tensor(lambda e, ops=self.b["tensor"]: [f(e) for f in ops])


def build_nc(debug=False):
    nc = bass.Bass("TRN2", target_bir_lowering=False)
    scratch_kind = "ExternalOutput" if debug else "Internal"

    xq_T = nc.dram_tensor("xq_T", [D, SQ], F32, kind="ExternalInput")
    xkv_T = nc.dram_tensor("xkv_T", [D, S], F32, kind="ExternalInput")
    x_own = nc.dram_tensor("x_own", [SQ, D], F32, kind="ExternalInput")
    wq_T = nc.dram_tensor("wq_T", [D, D], F32, kind="ExternalInput")
    wk_T = nc.dram_tensor("wk_T", [D, D], F32, kind="ExternalInput")
    wv_T = nc.dram_tensor("wv_T", [D, D], F32, kind="ExternalInput")
    wo_T = nc.dram_tensor("wo_T", [D, D], F32, kind="ExternalInput")
    wg_T = nc.dram_tensor("wg_T", [D, HF], I32, kind="ExternalInput")
    wu_T = nc.dram_tensor("wu_T", [D, HF], I32, kind="ExternalInput")
    wd_T = nc.dram_tensor("wd_T", [HF, D], I32, kind="ExternalInput")
    tri_e = nc.dram_tensor("tri_e", [128, 128], BF16, kind="ExternalInput")
    tri_o = nc.dram_tensor("tri_o", [128, 128], BF16, kind="ExternalInput")
    ident = nc.dram_tensor("ident", [128, 128], BF16, kind="ExternalInput")
    ones_col = nc.dram_tensor("ones_col", [128, 1], BF16, kind="ExternalInput")
    ones_row_f = nc.dram_tensor("ones_row_f", [1, 128], F32, kind="ExternalInput")
    negones_row_f = nc.dram_tensor("negones_row_f", [1, 128], F32,
                                   kind="ExternalInput")
    ones_row_b = nc.dram_tensor("ones_row_b", [65, 128], BF16, kind="ExternalInput")
    out = nc.dram_tensor("out", [SQ, D], F32, kind="ExternalOutput")

    kt_scr = nc.dram_tensor("kt_scr", [DC, 128, S], BF16, kind=scratch_kind)
    v_scr = nc.dram_tensor("v_scr", [128, 32, NH, 66], BF16, kind=scratch_kind)
    xnq_scr = nc.dram_tensor("xnq_scr", [128, DC, SQ], BF16, kind=scratch_kind)
    outT_scr = nc.dram_tensor("outT_scr", [128, DC, SQ], BF16, kind=scratch_kind)
    qt_scr = nc.dram_tensor("qt_scr", [128, DC, SQ], BF16, kind=scratch_kind)

    es = contextlib.ExitStack()
    with es:
        sMM = Sem(es, nc, "sMM")
        sVE = Sem(es, nc, "sVE")
        sSC = Sem(es, nc, "sSC")
        sCV = Sem(es, nc, "sCV")
        sCN = Sem(es, nc, "sCN")
        sxst = [Sem(es, nc, f"sxst{i}") for i in range(4)]
        swst = [Sem(es, nc, f"swst{i}") for i in range(2)]
        sSPX = [Sem(es, nc, f"sSPX{i}") for i in range(2)]
        sQS = Sem(es, nc, "sQS")
        skst = [Sem(es, nc, f"skst{i}") for i in range(2)]
        skh = [Sem(es, nc, f"skh{i}") for i in range(2)]
        sspo = [Sem(es, nc, f"sspo{i}") for i in range(2)]
        sOT = Sem(es, nc, "sOT")
        sXO = Sem(es, nc, "sXO")
        sXNR = Sem(es, nc, "sXNR")
        sist = [Sem(es, nc, f"sist{i}") for i in range(3)]
        sOUT = Sem(es, nc, "sOUT")
        sVS = Sem(es, nc, "sVS")
        sqt = [Sem(es, nc, f"sqt{i}") for i in range(2)]
        svs8 = [Sem(es, nc, f"svs8_{i}") for i in range(8)]

        c_tri_e = es.enter_context(nc.sbuf_tensor("c_tri_e", [128, 128], BF16))
        c_tri_o = es.enter_context(nc.sbuf_tensor("c_tri_o", [128, 128], BF16))
        c_ident = es.enter_context(nc.sbuf_tensor("c_ident", [128, 128], BF16))
        c_ones = es.enter_context(nc.sbuf_tensor("c_ones", [128, 1], BF16))
        c_ones_rf = es.enter_context(nc.sbuf_tensor("c_ones_rf", [1, 128], F32))
        c_negones_rf = es.enter_context(nc.sbuf_tensor("c_negones_rf", [1, 128], F32))
        c_ones_rb = es.enter_context(nc.sbuf_tensor("c_ones_rb", [65, 128], BF16))

        def ln_front(eb, xT_hbm, c, xstg, xbsq, p_s12, rows, state, src_c=None):
            sc = c if src_c is None else src_c
            rowsb = rows[c % 4]
            mrow, msqe, trow, lrow, arow, crow = (
                rowsb[0:1, CW * jj:CW * (jj + 1)] for jj in range(6))
            free_l = state.setdefault("xstg_free", [])
            fw = free_l[c - 3] if c - 3 >= 0 and len(free_l) > c - 3 else None
            fws = state.setdefault("sq_done", [])
            fw2 = fws[c - 3] if c - 3 >= 0 and len(fws) > c - 3 else None
            rows_l = state.setdefault("ve_rows", [])
            xn_l = state.setdefault("ve_xn", [])
            mmS_l = state.setdefault("mmS", [])
            cr_l = state.setdefault("ve_cr", [])
            s12c = p_s12[c % 2][0:1, 0:2 * CW]

            def ld(e, c=c, sc=sc, fw=fw, fw2=fw2):
                if fw is not None:
                    e.wait_ge(sSC.h, fw)
                if fw2 is not None:
                    e.wait_ge(sSC.h, fw2)
                for dc in range(DC):
                    e.dma_start(out=xstg[c % 3][:, dc, :],
                                in_=xT_hbm[128 * dc:128 * (dc + 1),
                                           CW * sc:CW * (sc + 1)]).then_inc(
                        sxst[c % 3].h, 16)
            eb.on("sync", ld)
            ld_done = sxst[c % 3].inc(16 * DC)

            def cp(e, c=c, t=ld_done,
                   tm=(xn_l[c - 5] if c >= 5 else None)):
                e.wait_ge(sxst[c % 3].h, t)
                if tm is not None:
                    e.wait_ge(sVE.h, tm)
                e.activation(xbsq[c % 5][:, :, 0:CW],
                             xstg[c % 3][:, :, :],
                             AF.Copy).then_inc(sSC.h, 1)
            eb.on("scalar", cp)
            free_l.append(sSC.inc())
            ve_xbf = free_l[-1]

            def sq(e, c=c, t=ld_done,
                   tm=(xn_l[c - 5] if c >= 5 else None)):
                e.wait_ge(sxst[c % 3].h, t)
                if tm is not None:
                    e.wait_ge(sVE.h, tm)
                e.activation(xbsq[c % 5][:, :, CW:2 * CW], xstg[c % 3][:, :, :],
                             AF.Square).then_inc(sSC.h, 1)
            eb.on("scalar", sq)
            fws.append(sSC.inc())
            sc_sq = fws[-1]

            def stats(e, c=c, t1=ve_xbf, t2=sc_sq, s12c=s12c,
                      tr=(rows_l[c - 2] if c >= 2 else None)):
                e.wait_ge(sSC.h, t1)
                e.wait_ge(sSC.h, t2)
                if tr is not None:
                    e.wait_ge(sVE.h, tr)
                for dc in range(DC):
                    ins = e.matmul(s12c, c_ones[:, :],
                                   xbsq[c % 5][:, dc, :],
                                   start=(dc == 0), stop=(dc == DC - 1))
                ins.then_inc(sMM.h, 1)
            eb.on("tensor", stats)
            mmS_l.append(sMM.inc())
            state["mmS_last"] = mmS_l[-1]

            def rows_f(e, c=c, t=mmS_l[-1], s12c=s12c,
                       tb=(state["bc_mm"][c - 2]
                           if c >= 2 and len(state.get("bc_mm", [])) > c - 2
                           else None)):
                e.wait_ge(sMM.h, t)
                if tb is not None:
                    e.wait_ge(sMM.h, tb)
                e.tensor_scalar_mul(mrow, s12c[0:1, 0:CW], 1.0 / D)
                e.tensor_tensor(msqe, mrow, mrow, ALU.mult)
                e.tensor_scalar_sub(msqe, msqe, EPS)
                e.scalar_tensor_tensor(trow, s12c[0:1, CW:2 * CW], 1.0 / D, msqe,
                                       ALU.mult, ALU.subtract).then_inc(sVE.h, 1)
            eb.on("vector", rows_f)
            rows_l.append(sVE.inc())
            state["ve_rows_last"] = rows_l[-1]
            state.setdefault("rows_mark", {})[c] = rows_l[-1]

        def ln_back(eb, c, dst_write, xstg, xbsq, p_s12, p_ab, rows, tmp_n,
                    state, norm_extra_wait=None):
            rowsb = rows[c % 4]
            mrow = rowsb[0:1, 0:CW]
            trow = rowsb[0:1, 2 * CW:3 * CW]
            lrow = rowsb[0:1, 3 * CW:4 * CW]
            arow = rowsb[0:1, 4 * CW:5 * CW]
            crow = rowsb[0:1, 5 * CW:6 * CW]
            ab = p_ab[c % 2]
            xn_l = state["ve_xn"]
            bc_l = state.setdefault("bc_mm", [])
            cr_l = state["ve_cr"]

            def rstd(e, t=state["rows_mark"][c]):
                e.wait_ge(sVE.h, t)
                e.activation(lrow, trow, AF.Ln)
                e.activation(arow, lrow, AF.Exp, scale=-0.5).then_inc(sSC.h, 1)
            eb.on("scalar", rstd)
            sc_r = sSC.inc()

            def bc(e, c=c, t=sc_r, ab=ab,
                   tn=(xn_l[c - 2] if c >= 2 else None)):
                e.wait_ge(sSC.h, t)
                if tn is not None:
                    e.wait_ge(sVE.h, tn)
                e.matmul(ab[:, 0:CW], c_ones_rf[:, :], arow, start=True, stop=True)
                e.matmul(ab[:, CW:2 * CW], c_ones_rf[:, :], mrow, start=True,
                         stop=True).then_inc(sMM.h, 1)
            eb.on("tensor", bc)
            bc_l.append(sMM.inc())
            mm_bc = bc_l[-1]

            def norm(e, c=c, t=mm_bc, ab=ab, xw=norm_extra_wait, dw=dst_write):
                e.wait_ge(sMM.h, t)
                for sem_h, val in (xw or ()):
                    e.wait_ge(sem_h, val)
                for dc in range(DC):
                    e.tensor_tensor(tmp_n[:, :], xbsq[c % 5][:, dc, 0:CW],
                                    ab[:, CW:2 * CW], ALU.subtract)
                    ins = e.tensor_tensor(dw(dc), tmp_n[:, :], ab[:, 0:CW],
                                          ALU.mult)
                ins.then_inc(sVE.h, 1)
            eb.on("vector", norm)
            t_xn = sVE.inc()
            xn_l.append(t_xn)
            state["ve_xn_last"] = t_xn
            return t_xn

        def stream_weights(eb, pieces, wstg, rot, ssl):
            marks = []
            for (src, dst, width, scale, xw) in pieces:
                slot = rot["i"]
                rot["i"] += 1

                def ld(e, src=src, slot=slot, width=width, xw=xw):
                    if slot >= len(wstg):
                        e.wait_ge(sSC.h, rot["marks"][slot - len(wstg)])
                    if xw is not None:
                        e.wait_ge(xw[0], xw[1])
                    e.dma_start(out=wstg[slot % len(wstg)][:, 0:width],
                                in_=src).then_inc(ssl[slot % len(wstg)].h, 16)
                eb.on("sync", ld)
                t = ssl[slot % len(wstg)].inc(16)

                def cv(e, dst=dst, slot=slot, width=width, scale=scale, t=t,
                       sh=ssl[slot % len(wstg)].h):
                    e.wait_ge(sh, t)
                    if scale is None:
                        e.activation(dst, wstg[slot % len(wstg)][:, 0:width],
                                     AF.Copy).then_inc(sSC.h, 1)
                    else:
                        e.activation(dst, wstg[slot % len(wstg)][:, 0:width],
                                     AF.Copy, scale=scale).then_inc(sSC.h, 1)
                eb.on("scalar", cv)
                m = sSC.inc()
                rot["marks"].append(m)
                marks.append(m)
            return marks

        with contextlib.ExitStack() as sA:
            v_sb = sA.enter_context(nc.sbuf_tensor("v_sb", [128, 8, NH, 66], BF16))
            with contextlib.ExitStack() as sL:
                wqb = sL.enter_context(nc.sbuf_tensor("wqb", [128, DC, 1024], BF16))
                wkb = sL.enter_context(nc.sbuf_tensor("wkb", [128, DC, 1024], BF16))
                wvb = sL.enter_context(nc.sbuf_tensor("wvb", [128, DC, 1024], BF16))
                xnqb = [sL.enter_context(nc.sbuf_tensor(f"xnqb{i}", [128, DC, 512],
                                                        BF16))
                        for i in range(2)]
                xnkv = [sL.enter_context(nc.sbuf_tensor(f"xnkv{i}", [128, DC, 512],
                                                        BF16))
                        for i in range(3)]
                xstg = [sL.enter_context(nc.sbuf_tensor(f"xstg{i}", [128, DC, CW],
                                                        F32))
                        for i in range(3)]
                xbsq = [sL.enter_context(nc.sbuf_tensor(f"xbsq{i}",
                                                        [128, DC, 2 * CW], BF16))
                        for i in range(5)]
                rows = [sL.enter_context(nc.sbuf_tensor(f"rows{i}", [1, 6 * CW],
                                                        F32))
                        for i in range(4)]
                tmp_n = sL.enter_context(nc.sbuf_tensor("tmp_n", [128, CW], F32))
                kstg = [sL.enter_context(nc.sbuf_tensor(f"kstg{i}", [128, 512],
                                                        BF16))
                        for i in range(4)]
                qtstg = [sL.enter_context(nc.sbuf_tensor(f"qtstg{i}", [128, 1024],
                                                         BF16))
                         for i in range(2)]
                wstg = [sL.enter_context(nc.sbuf_tensor(f"wstgL{i}", [128, 512],
                                                        F32))
                        for i in range(2)]

                with contextlib.ExitStack() as psL:
                    p_s12 = [psL.enter_context(nc.psum_tensor(f"p_s12_{i}",
                                                              [1, 2 * CW], F32))
                             for i in range(2)]
                    p_ab = [psL.enter_context(nc.psum_tensor(f"p_ab{i}",
                                                             [128, 2 * CW], F32))
                            for i in range(2)]
                    pool = [psL.enter_context(nc.psum_tensor(f"pool{i}",
                                                             [128, 512], F32))
                            for i in range(4)]

                    eb = EngBufs()

                    def ld_consts(e):
                        for dst, srcd in ((c_tri_e, tri_e), (c_tri_o, tri_o),
                                          (c_ident, ident), (c_ones, ones_col),
                                          (c_ones_rf, ones_row_f),
                                          (c_negones_rf, negones_row_f),
                                          (c_ones_rb, ones_row_b)):
                            e.dma_start(out=dst[:, :], in_=srcd[:, :]).then_inc(
                                sCN.h, 16)
                    eb.on("sync", ld_consts)
                    t_consts = sCN.inc(16 * 7)
                    eb.on("tensor", lambda e, t=t_consts: e.wait_ge(sCN.h, t))
                    eb.on("vector", lambda e: e.memset(v_sb[:, :, :, 64:65], 1.0))

                    st0 = {}
                    cg = [0]
                    xnq_marks = {}
                    xnkv_marks = {}
                    kvlast = []
                    q_last = []
                    spx_marks = []

                    def emit_front(kind, idx):
                        c = cg[0]
                        cg[0] += 1
                        src = xkv_T if kind == "kv" else xq_T
                        ln_front(eb, src, c, xstg, xbsq, p_s12, rows, st0,
                                 src_c=idx)
                        return c

                    def emit_back(kind, idx, c):
                        if kind == "kv":
                            hp = idx // 2 - 3
                            xw = ([(sMM.h, kvlast[hp // 2])] if hp >= 0
                                  else None)
                            dst = (lambda dc, k=idx:
                                   xnkv[(k // 2) % 3][:, dc,
                                                      CW * (k % 2):
                                                      CW * (k % 2 + 1)])
                            xnkv_marks[idx] = ln_back(
                                eb, c, dst, xstg, xbsq, p_s12, p_ab, rows,
                                tmp_n, st0, norm_extra_wait=xw)
                        else:
                            p = idx // 2
                            xw = []
                            if p >= 2:
                                xw.append((sMM.h, q_last[p - 2]))
                                xw.append((sSPX[p % 2].h, spx_marks[p - 2]))
                            dst = (lambda dc, o=idx:
                                   xnqb[(o // 2) % 2][:, dc,
                                                      CW * (o % 2):
                                                      CW * (o % 2 + 1)])
                            xnq_marks[idx] = ln_back(
                                eb, c, dst, xstg, xbsq, p_s12, p_ab, rows,
                                tmp_n, st0, norm_extra_wait=xw)

                    rotL = {"i": 0, "marks": []}

                    def wpieces(w_T, wb):
                        return [(w_T[128 * (i // 2):128 * (i // 2 + 1),
                                     512 * (i % 2):512 * (i % 2 + 1)],
                                 wb[:, i // 2, 512 * (i % 2):512 * (i % 2 + 1)],
                                 512, None, None) for i in range(2 * DC)]

                    for kind, idx in (("kv", 0), ("kv", 1)):
                        c = emit_front(kind, idx)
                        emit_back(kind, idx, c)
                    wk_done = stream_weights(eb, wpieces(wk_T, wkb), wstg, rotL,
                                             swst)[-1]
                    for kind, idx in (("kv", 2), ("kv", 3)):
                        c = emit_front(kind, idx)
                        emit_back(kind, idx, c)
                    wv_done = stream_weights(eb, wpieces(wv_T, wvb), wstg, rotL,
                                             swst)[-1]
                    for kind, idx in (("own", 0), ("own", 1)):
                        c = emit_front(kind, idx)
                        emit_back(kind, idx, c)
                    wq_done = stream_weights(eb, wpieces(wq_T, wqb), wstg, rotL,
                                             swst)[-1]

                    up = [0]
                    pool_ev = []
                    kst_marks = []
                    kcnt = [0]
                    qsp_marks = []
                    qcnt = [0]
                    vcnt = [0]

                    def emit_k_unit(g, et, t_xn):
                        u = up[0]
                        up[0] += 1
                        ki = kcnt[0]
                        kcnt[0] += 1

                        def kmm(e, g=g, et=et, u=u, t=t_xn,
                                peu=(pool_ev[u - 2] if u >= 2 else None),
                                tw=(wk_done if (g == 0 and et == 0) else None)):
                            e.wait_ge(sVE.h, t)
                            if tw is not None:
                                e.wait_ge(sSC.h, tw)
                            if peu is not None:
                                e.wait_ge(sVE.h, peu)
                            prs = ((2 * g) % 3, (2 * g + 1) % 3)
                            for dc in range(DC):
                                for pp in range(2):
                                    ins = e.matmul(
                                        pool[2 * (u % 2) + pp][:, :],
                                        wkb[:, dc, 128 * et:128 * (et + 1)],
                                        xnkv[prs[pp]][:, dc, :],
                                        start=(dc == 0), stop=(dc == DC - 1))
                            ins.then_inc(sMM.h, 1)
                        eb.on("tensor", kmm)
                        mmk = sMM.inc()

                        def kev(e, u=u, ki=ki, t=mmk,
                                ts=(kst_marks[ki - 2] if ki >= 2 else None)):
                            e.wait_ge(sMM.h, t)
                            if ts is not None:
                                e.wait_ge(skst[ki % 2].h, ts)
                            e.tensor_copy(kstg[2 * (ki % 2)][:, :],
                                          pool[2 * (u % 2)][:, :])
                            e.tensor_copy(kstg[2 * (ki % 2) + 1][:, :],
                                          pool[2 * (u % 2) + 1][:, :]
                                          ).then_inc(sVE.h, 1)
                        eb.on("vector", kev)
                        pool_ev.append(sVE.inc())

                        def kst(e, g=g, et=et, ki=ki, t=pool_ev[-1]):
                            e.wait_ge(sVE.h, t)
                            e.dma_start(
                                out=kt_scr[et, :, 1024 * g:1024 * g + 512],
                                in_=kstg[2 * (ki % 2)][:, :]).then_inc(
                                skst[ki % 2].h, 16)
                            e.dma_start(
                                out=kt_scr[et, :, 1024 * g + 512:1024 * (g + 1)],
                                in_=kstg[2 * (ki % 2) + 1][:, :]).then_inc(
                                skst[ki % 2].h, 16)
                        eb.on("gpsimd", kst)
                        kst_marks.append(skst[ki % 2].inc(32))

                    def emit_v_unit(g, sti, t_xn):
                        u = up[0]
                        up[0] += 1
                        st = 8 * g + sti
                        vcnt[0] += 1

                        def vmm(e, g=g, sti=sti, u=u, t=t_xn,
                                peu=(pool_ev[u - 2] if u >= 2 else None),
                                tw=(wv_done if (g == 0 and sti == 0) else None)):
                            e.wait_ge(sVE.h, t)
                            if tw is not None:
                                e.wait_ge(sSC.h, tw)
                            if peu is not None:
                                e.wait_ge(sVE.h, peu)
                            prs = ((2 * g) % 3, (2 * g + 1) % 3)
                            for dc in range(DC):
                                lhs = xnkv[prs[sti // 4]][:, dc,
                                                          128 * (sti % 4):
                                                          128 * (sti % 4 + 1)]
                                e.matmul(pool[2 * (u % 2)][:, :], lhs,
                                         wvb[:, dc, 0:512],
                                         start=(dc == 0), stop=(dc == DC - 1))
                                ins = e.matmul(pool[2 * (u % 2) + 1][:, :],
                                               lhs, wvb[:, dc, 512:1024],
                                               start=(dc == 0),
                                               stop=(dc == DC - 1))
                            ins.then_inc(sMM.h, 1)
                        eb.on("tensor", vmm)
                        mmv = sMM.inc()

                        def vev(e, st=st, u=u, t=mmv,
                                tvs=(svs8[st % 8].n if st >= 8 else None)):
                            e.wait_ge(sMM.h, t)
                            if tvs is not None:
                                e.wait_ge(svs8[st % 8].h, tvs)
                            for half in range(2):
                                src = pool[2 * (u % 2) + half].ap().rearrange(
                                    "p (h c) -> p h c", h=8, c=64)
                                ins = e.tensor_copy(
                                    v_sb[:, st % 8,
                                         8 * half:8 * (half + 1), 0:64],
                                    src)
                            ins.then_inc(sVE.h, 1)
                        eb.on("vector", vev)
                        pool_ev.append(sVE.inc())

                        def vsp(e, st=st, t=pool_ev[-1]):
                            e.wait_ge(sVE.h, t)
                            e.dma_start(out=v_scr[:, st, :, :],
                                        in_=v_sb[:, st % 8, :, :]).then_inc(
                                svs8[st % 8].h, 16)
                        eb.on("gpsimd", vsp)
                        svs8[st % 8].inc(16)

                    def emit_q_unit(p):
                        t_xn = xnq_marks[2 * p + 1]
                        for ep in range(4):
                            u = up[0]
                            up[0] += 1
                            qi = qcnt[0]
                            qcnt[0] += 1

                            def qmm(e, p=p, ep=ep, u=u, t=t_xn,
                                    peu=(pool_ev[u - 2] if u >= 2 else None),
                                    tw=(wq_done if (p == 0 and ep == 0)
                                        else None)):
                                e.wait_ge(sVE.h, t)
                                if tw is not None:
                                    e.wait_ge(sSC.h, tw)
                                if peu is not None:
                                    e.wait_ge(sVE.h, peu)
                                for dc in range(DC):
                                    for half in range(2):
                                        ins = e.matmul(
                                            pool[2 * (u % 2) + half][:, :],
                                            wqb[:, dc,
                                                128 * (2 * ep + half):
                                                128 * (2 * ep + half + 1)],
                                            xnqb[p % 2][:, dc, :],
                                            start=(dc == 0),
                                            stop=(dc == DC - 1))
                                ins.then_inc(sMM.h, 1)
                            eb.on("tensor", qmm)
                            mmq = sMM.inc()

                            def qev(e, u=u, qi=qi, t=mmq,
                                    ts=(qsp_marks[qi - 2] if qi >= 2 else None)):
                                e.wait_ge(sMM.h, t)
                                if ts is not None:
                                    e.wait_ge(sqt[qi % 2].h, ts)
                                e.tensor_copy(qtstg[qi % 2][:, 0:512],
                                              pool[2 * (u % 2)][:, :])
                                e.tensor_copy(qtstg[qi % 2][:, 512:1024],
                                              pool[2 * (u % 2) + 1][:, :]
                                              ).then_inc(sVE.h, 1)
                            eb.on("vector", qev)
                            pool_ev.append(sVE.inc())

                            def qsp(e, p=p, ep=ep, qi=qi, t=pool_ev[-1]):
                                e.wait_ge(sVE.h, t)
                                e.dma_start(
                                    out=qt_scr[:, 2 * ep, 512 * p:512 * (p + 1)],
                                    in_=qtstg[qi % 2][:, 0:512]).then_inc(
                                    sqt[qi % 2].h, 16)
                                e.dma_start(
                                    out=qt_scr[:, 2 * ep + 1,
                                               512 * p:512 * (p + 1)],
                                    in_=qtstg[qi % 2][:, 512:1024]).then_inc(
                                    sqt[qi % 2].h, 16)
                            eb.on("gpsimd", qsp)
                            qsp_marks.append(sqt[qi % 2].inc(32))
                        q_last.append(sMM.n)

                        def spillx(e, p=p, txn=t_xn):
                            e.wait_ge(sVE.h, txn)
                            e.dma_start(out=xnq_scr[:, :, 512 * p:512 * (p + 1)],
                                        in_=xnqb[p % 2][:, :, :]).then_inc(
                                sSPX[p % 2].h, 16)
                        eb.on("gpsimd", spillx)
                        spx_marks.append(sSPX[p % 2].inc(16))

                    for g in range(S // 1024):
                        w1 = [("own", 2 * g + 2), ("kv", 4 * g + 4),
                              ("kv", 4 * g + 5)]
                        w1 = [(kk, ii) for kk, ii in w1
                              if (ii < 16 if kk == "kv" else ii < 8)]
                        c1 = [emit_front(kk, ii) for kk, ii in w1]
                        t_xn = xnkv_marks[4 * g + 3]
                        for et in range(DC):
                            emit_k_unit(g, et, t_xn)
                        for (kk, ii), c in zip(w1, c1):
                            emit_back(kk, ii, c)

                        w2 = [("kv", 4 * g + 6), ("kv", 4 * g + 7),
                              ("own", 2 * g + 3)]
                        w2 = [(kk, ii) for kk, ii in w2
                              if (ii < 16 if kk == "kv" else ii < 8)]
                        c2 = [emit_front(kk, ii) for kk, ii in w2]
                        for sti in range(8):
                            emit_v_unit(g, sti, t_xn)
                        emit_q_unit(g)
                        kvlast.append(sMM.n)
                        for (kk, ii), c in zip(w2, c2):
                            emit_back(kk, ii, c)

                    def ldrain(e, t0=skst[0].n, t1=skst[1].n,
                               tvs=tuple(s.n for s in svs8),
                               tq0=sqt[0].n, tq1=sqt[1].n,
                               ts0=sSPX[0].n, ts1=sSPX[1].n):
                        e.wait_ge(skst[0].h, t0)
                        e.wait_ge(skst[1].h, t1)
                        for i in range(8):
                            e.wait_ge(svs8[i].h, tvs[i])
                        e.wait_ge(sqt[0].h, tq0)
                        e.wait_ge(sqt[1].h, tq1)
                        e.wait_ge(sSPX[0].h, ts0)
                        e.wait_ge(sSPX[1].h, ts1)
                    eb.on("gpsimd", ldrain)
                    eb.flush(nc, "pL")

        with contextlib.ExitStack() as sZ:
            acc = sZ.enter_context(nc.sbuf_tensor("acc", [128, 16, 1024], F32))

            with contextlib.ExitStack() as sM:
                xnq_r = sM.enter_context(nc.sbuf_tensor("xnq_r", [128, DC, SQ], BF16))
                h_sb = sM.enter_context(nc.sbuf_tensor("h_sb", [128, 2, SQ], BF16))
                kh = [sM.enter_context(nc.sbuf_tensor(f"kh{i}", [128, S], BF16))
                      for i in range(2)]
                qh = [sM.enter_context(nc.sbuf_tensor(f"qh{i}", [128, SQ], BF16))
                      for i in range(2)]
                vp = [sM.enter_context(nc.sbuf_tensor(f"vp{i}", [128, 32, 2, 66],
                                                      BF16))
                      for i in range(2)]
                pt = [sM.enter_context(nc.sbuf_tensor(f"pt{i}", [128, 1024], BF16))
                      for i in range(6)]
                osb = sM.enter_context(nc.sbuf_tensor("osb", [64, 1024], BF16))
                otst = [sM.enter_context(nc.sbuf_tensor(f"otst{i}", [64, 1024], BF16))
                        for i in range(2)]
                dinv = sM.enter_context(nc.sbuf_tensor("dinv", [65, 1024], BF16))
                wgb2 = [sM.enter_context(nc.sbuf_tensor(f"wgb2_{i}",
                                                        [128, DC, SLAB2], BF16))
                        for i in range(2)]
                wub2 = [sM.enter_context(nc.sbuf_tensor(f"wub2_{i}",
                                                        [128, DC, SLAB2], BF16))
                        for i in range(2)]
                wdb2 = [sM.enter_context(nc.sbuf_tensor(f"wdb2_{i}",
                                                        [128, 2, 1024], BF16))
                        for i in range(2)]
                istg = [sM.enter_context(nc.sbuf_tensor(f"istg{i}", [128, 512], I32))
                        for i in range(3)]
                tsil = [sM.enter_context(nc.sbuf_tensor(f"tsil{i}", [128, 1024], F32))
                        for i in range(2)]

                with contextlib.ExitStack() as psM:
                    p_sp = [psM.enter_context(nc.psum_tensor(f"p_sp{i}",
                                                             [128, 1024], F32))
                            for i in range(2)]
                    p_O = psM.enter_context(nc.psum_tensor("p_O", [128, 1024], F32))
                    p_F = psM.enter_context(nc.psum_tensor("p_F", [128, 1024], F32))

                    eb = EngBufs()
                    NPAIR = NH // 2
                    khload = [None] * NPAIR

                    for pr0 in range(2):
                        def kld0(e, pr0=pr0):
                            e.dma_start(out=kh[pr0 % 2][:, :],
                                        in_=kt_scr[pr0, :, :]).then_inc(
                                skh[pr0 % 2].h, 16)
                            e.dma_start(out=qh[pr0 % 2][:, :],
                                        in_=qt_scr[:, pr0, :]).then_inc(
                                skh[pr0 % 2].h, 16)
                            e.dma_start(out=vp[pr0 % 2][:, :, :, :],
                                        in_=v_scr[:, :, 2 * pr0:2 * pr0 + 2, :]
                                        ).then_inc(skh[pr0 % 2].h, 16)
                        eb.on("sync", kld0)
                        khload[pr0] = skh[pr0 % 2].inc(48)

                    def ld_xn(e):
                        e.dma_start(out=xnq_r[:, :, :],
                                    in_=xnq_scr[:, :, :]).then_inc(sXNR.h, 16)
                    eb.on("sync", ld_xn)
                    t_xnr = sXNR.inc(16)

                    for st in range(16):
                        def ld_xo(e, st=st):
                            e.dma_start(out=acc[:, st, :],
                                        in_=x_own[128 * st:128 * (st + 1), :]
                                        ).then_inc(sXO.h, 16)
                        eb.on("gpsimd", ld_xo)
                        sXO.inc(16)
                    t_xo_all = sXO.n

                    cv_done2 = [None] * NSLAB2
                    down_done2 = [None] * NSLAB2
                    slot_ctr = [0]

                    def emit_slab_load2(m):
                        pieces = []
                        for dc in range(DC):
                            pieces.append((wg_T[128 * dc:128 * (dc + 1),
                                                SLAB2 * m:SLAB2 * (m + 1)],
                                           wgb2[m % 2][:, dc, :], SLAB2))
                        for dc in range(DC):
                            pieces.append((wu_T[128 * dc:128 * (dc + 1),
                                                SLAB2 * m:SLAB2 * (m + 1)],
                                           wub2[m % 2][:, dc, :], SLAB2))
                        for r in range(2):
                            for hb in range(2):
                                pieces.append(
                                    (wd_T[SLAB2 * m + 128 * r:
                                          SLAB2 * m + 128 * (r + 1),
                                          512 * hb:512 * (hb + 1)],
                                     wdb2[m % 2][:, r, 512 * hb:512 * (hb + 1)],
                                     512))
                        cvs = []
                        for pi, (srcp, dst, width) in enumerate(pieces):
                            slot = slot_ctr[0]
                            slot_ctr[0] += 1

                            def ld(e, srcp=srcp, slot=slot, width=width,
                                   td=(down_done2[m - 2]
                                       if (m >= 2 and pi == 0) else None)):
                                if slot >= 3:
                                    e.wait_ge(sCV.h, slot - 2)
                                if td is not None:
                                    e.wait_ge(sMM.h, td)
                                e.dma_start(out=istg[slot % 3][:, 0:width],
                                            in_=srcp).then_inc(
                                    sist[slot % 3].h, 16)
                            eb.on("sync", ld)
                            t = sist[slot % 3].inc(16)

                            def cv(e, dst=dst, slot=slot, width=width, t=t):
                                e.wait_ge(sist[slot % 3].h, t)
                                e.tensor_scalar_mul(
                                    dst, istg[slot % 3][:, 0:width],
                                    1.0 / 127.0).then_inc(sCV.h, 1)
                            cvs.append(cv)
                        cv_done2[m] = slot_ctr[0]
                        return cvs

                    pending_cvs = []
                    popped = [0]

                    def pop_cv(k):
                        for _ in range(k):
                            if pending_cvs:
                                eb.on("vector", pending_cvs.pop(0))
                                popped[0] += 1

                    def pop_cv_until(n):
                        while popped[0] < n and pending_cvs:
                            eb.on("vector", pending_cvs.pop(0))
                            popped[0] += 1

                    pending_cvs.extend(emit_slab_load2(0))
                    pending_cvs.extend(emit_slab_load2(1))

                    def ffn_gen():
                        hm = {}
                        ve_yev = {}
                        for m in range(NSLAB2):
                            pop_cv_until(cv_done2[m])
                            for t in range(2):
                                for sh in range(2):
                                    first = (t == 0 and sh == 0)
                                    if first:
                                        tff = ve_yev.get((m - 1, 15))
                                    elif sh == 1:
                                        tff = hm[(m, t, 0)]
                                    else:
                                        tff = hm[(m, t - 1, 1)]

                                    def gmm(e, m=m, t=t, sh=sh, tff=tff,
                                            tc=(cv_done2[m] if first else None),
                                            txn=(t_xnr if (m == 0 and first)
                                                 else None)):
                                        if txn is not None:
                                            e.wait_ge(sXNR.h, txn)
                                        if tc is not None:
                                            e.wait_ge(sCV.h, tc)
                                        if tff is not None:
                                            e.wait_ge(sVE.h, tff[0])
                                        for u in range(2):
                                            if u == 1 and tff is not None:
                                                e.wait_ge(sVE.h, tff[1])
                                            for dc in range(DC):
                                                ins = e.matmul(
                                                    p_F[:, 512 * u:512 * (u + 1)],
                                                    wgb2[m % 2][:, dc,
                                                                128 * t:
                                                                128 * (t + 1)],
                                                    xnq_r[:, dc,
                                                          1024 * sh + 512 * u:
                                                          1024 * sh +
                                                          512 * (u + 1)],
                                                    start=(dc == 0),
                                                    stop=(dc == DC - 1))
                                            ins.then_inc(sMM.h, 1)
                                    eb.on("tensor", gmm)
                                    gA = sMM.inc()
                                    gB = sMM.inc()

                                    tsp = (hm.get((m, t - 1, sh)) if t >= 1
                                           else hm.get((m - 1, 1, sh)))

                                    def silf(e, sh=sh, gA=gA, gB=gB, tsp=tsp):
                                        e.wait_ge(sMM.h, gA)
                                        if tsp is not None:
                                            e.wait_ge(sVE.h, tsp[0])
                                        e.activation(tsil[sh][:, 0:512],
                                                     p_F[:, 0:512],
                                                     AF.Silu).then_inc(sSC.h, 1)
                                        e.wait_ge(sMM.h, gB)
                                        if tsp is not None:
                                            e.wait_ge(sVE.h, tsp[1])
                                        e.activation(tsil[sh][:, 512:1024],
                                                     p_F[:, 512:1024],
                                                     AF.Silu).then_inc(sSC.h, 1)
                                    eb.on("scalar", silf)
                                    silA = sSC.inc()
                                    silB = sSC.inc()
                                    pop_cv(1)
                                    yield 5.1

                                    def umm(e, m=m, t=t, sh=sh,
                                            silA=silA, silB=silB):
                                        for u in range(2):
                                            e.wait_ge(sSC.h,
                                                      silA if u == 0 else silB)
                                            for dc in range(DC):
                                                ins = e.matmul(
                                                    p_F[:, 512 * u:512 * (u + 1)],
                                                    wub2[m % 2][:, dc,
                                                                128 * t:
                                                                128 * (t + 1)],
                                                    xnq_r[:, dc,
                                                          1024 * sh + 512 * u:
                                                          1024 * sh +
                                                          512 * (u + 1)],
                                                    start=(dc == 0),
                                                    stop=(dc == DC - 1))
                                            ins.then_inc(sMM.h, 1)
                                    eb.on("tensor", umm)
                                    uA = sMM.inc()
                                    uB = sMM.inc()

                                    def hmf(e, m=m, t=t, sh=sh, uA=uA, uB=uB,
                                            td=(down_done2[m - 1]
                                                if (m >= 1 and t == 0 and sh == 0)
                                                else None)):
                                        e.wait_ge(sMM.h, uA)
                                        if td is not None:
                                            e.wait_ge(sMM.h, td)
                                        e.tensor_tensor(
                                            h_sb[:, t,
                                                 1024 * sh:1024 * sh + 512],
                                            tsil[sh][:, 0:512], p_F[:, 0:512],
                                            ALU.mult).then_inc(sVE.h, 1)
                                        e.wait_ge(sMM.h, uB)
                                        e.tensor_tensor(
                                            h_sb[:, t,
                                                 1024 * sh + 512:
                                                 1024 * (sh + 1)],
                                            tsil[sh][:, 512:1024],
                                            p_F[:, 512:1024],
                                            ALU.mult).then_inc(sVE.h, 1)
                                    eb.on("vector", hmf)
                                    hmA = sVE.inc()
                                    hmB = sVE.inc()
                                    hm[(m, t, sh)] = (hmA, hmB)
                                    pop_cv(1)
                                    yield 5.1

                            for st in range(16):
                                def dmm(e, m=m, st=st,
                                        th=(hm[(m, 1, 1)] if st == 0
                                            else None),
                                        typ=(ve_yev[(m, st - 1)] if st >= 1
                                             else None)):
                                    if th is not None:
                                        e.wait_ge(sVE.h, th[1])
                                    for nb in range(2):
                                        if typ is not None:
                                            e.wait_ge(sVE.h, typ[nb])
                                        for hc in range(2):
                                            ins = e.matmul(
                                                p_F[:, 512 * nb:
                                                    512 * (nb + 1)],
                                                h_sb[:, hc,
                                                     128 * st:128 * (st + 1)],
                                                wdb2[m % 2][:, hc,
                                                            512 * nb:
                                                            512 * (nb + 1)],
                                                start=(hc == 0),
                                                stop=(hc == 1))
                                        ins.then_inc(sMM.h, 1)
                                eb.on("tensor", dmm)
                                dA = sMM.inc()
                                dB = sMM.inc()
                                if st == 15:
                                    down_done2[m] = dB

                                def yevf(e, st=st, dA=dA, dB=dB,
                                         tx=(t_xo_all
                                             if (m == 0 and st == 0)
                                             else None)):
                                    e.wait_ge(sMM.h, dA)
                                    if tx is not None:
                                        e.wait_ge(sXO.h, tx)
                                    e.tensor_tensor(
                                        acc[:, st, 0:512], acc[:, st, 0:512],
                                        p_F[:, 0:512],
                                        ALU.add).then_inc(sVE.h, 1)
                                    e.wait_ge(sMM.h, dB)
                                    e.tensor_tensor(
                                        acc[:, st, 512:1024],
                                        acc[:, st, 512:1024],
                                        p_F[:, 512:1024],
                                        ALU.add).then_inc(sVE.h, 1)
                                eb.on("vector", yevf)
                                yA = sVE.inc()
                                yB = sVE.inc()
                                ve_yev[(m, st)] = (yA, yB)
                                pop_cv(1)
                                yield 1.35
                            if m + 2 < NSLAB2:
                                pending_cvs.extend(emit_slab_load2(m + 2))

                    ffn_it = ffn_gen()
                    ffn = {"tm": 0.0, "done": False}

                    def ffn_step(target):
                        while not ffn["done"] and ffn["tm"] < target:
                            try:
                                ffn["tm"] += next(ffn_it)
                            except StopIteration:
                                ffn["done"] = True

                    att = {"tm": 0.0}
                    FFN_DELAY = 65.0
                    FRATIO = 1.12

                    def maybe_ffn():
                        if (att["tm"] > FFN_DELAY and not ffn["done"]
                                and ffn["tm"] < (att["tm"] - FFN_DELAY) * FRATIO):
                            try:
                                ffn["tm"] += next(ffn_it)
                            except StopIteration:
                                ffn["done"] = True

                    smarkj = []
                    scpj = []
                    pvj = []
                    scd = []
                    dbmark = []
                    div_ve = []
                    spill_m = []

                    def emit_division(j, tse):
                        def dbmm(e, j=j, t=scd[j], tse=tse,
                                 tv=(div_ve[j - 1][1] if j >= 1 else None)):
                            e.wait_ge(sSC.h, tse if tse is not None else t)
                            if tv is not None:
                                e.wait_ge(sVE.h, tv)
                            e.matmul(p_sp[0][0:64, 0:512],
                                     c_ones_rb[64:65, 0:64],
                                     dinv[64:65, 0:512], start=True,
                                     stop=True).then_inc(sMM.h, 1)
                            e.matmul(p_sp[0][0:64, 512:1024],
                                     c_ones_rb[64:65, 0:64],
                                     dinv[64:65, 512:1024], start=True,
                                     stop=True).then_inc(sMM.h, 1)
                        eb.on("tensor", dbmm)
                        dbA = sMM.inc()
                        dbB = sMM.inc()
                        dbmark.append(dbB)

                        def dv(e, j=j, dbA=dbA, dbB=dbB,
                               ts=(spill_m[j - 2] if j >= 2 else None)):
                            e.wait_ge(sMM.h, dbA)
                            if ts is not None:
                                e.wait_ge(sspo[j % 2].h, ts)
                            e.tensor_copy(osb[:, :], p_O[0:64, :])
                            e.tensor_tensor(otst[j % 2][:, 0:512],
                                            osb[:, 0:512],
                                            p_sp[0][0:64, 0:512],
                                            ALU.mult).then_inc(sVE.h, 1)
                            e.wait_ge(sMM.h, dbB)
                            e.tensor_tensor(otst[j % 2][:, 512:1024],
                                            osb[:, 512:1024],
                                            p_sp[0][0:64, 512:1024],
                                            ALU.mult).then_inc(sVE.h, 1)
                        eb.on("vector", dv)
                        dvA = sVE.inc()
                        dvB = sVE.inc()
                        div_ve.append((dvA, dvB))

                        def sp_dma(e, j=j, t=div_ve[-1][1]):
                            pr, qb = j // 4, j % 4
                            e.wait_ge(sVE.h, t)
                            e.dma_start(
                                out=outT_scr[0:64, pr, 512 * qb:512 * (qb + 1)],
                                in_=otst[j % 2][:, 0:512]).then_inc(
                                sspo[j % 2].h, 16)
                            e.dma_start(
                                out=outT_scr[64:128, pr,
                                             512 * qb:512 * (qb + 1)],
                                in_=otst[j % 2][:, 512:1024]).then_inc(
                                sspo[j % 2].h, 16)
                        eb.on("gpsimd", sp_dma)
                        spill_m.append(sspo[j % 2].inc(32))
                        att["tm"] += 0.8

                    for j in range(4 * NPAIR):
                        pr, qb = j // 4, j % 4
                        nkt = 8 * qb + 8
                        khb = kh[pr % 2]
                        qcol = 512 * qb

                        if qb == 0 and 1 <= pr < NPAIR - 1:
                            def kld(e, pr1=pr + 1, tm=pvj[4 * pr - 1][-1]):
                                e.wait_ge(sMM.h, tm)
                                e.dma_start(out=kh[pr1 % 2][:, :],
                                            in_=kt_scr[pr1, :, :]).then_inc(
                                    skh[pr1 % 2].h, 16)
                                e.dma_start(out=qh[pr1 % 2][:, :],
                                            in_=qt_scr[:, pr1, :]).then_inc(
                                    skh[pr1 % 2].h, 16)
                                e.dma_start(out=vp[pr1 % 2][:, :, :, :],
                                            in_=v_scr[:, :,
                                                      2 * pr1:2 * pr1 + 2, :]
                                            ).then_inc(skh[pr1 % 2].h, 16)
                            eb.on("scalar", kld)
                            khload[pr + 1] = skh[(pr + 1) % 2].inc(48)

                        smark = []
                        scpA = []
                        scpB = []
                        pvm = []
                        smarkj.append(smark)
                        scpj.append(scpB)
                        pvj.append(pvm)

                        def emit_pv(ktx, pr=pr, nkt=nkt, pvm=pvm, scpB=scpB):
                            band = ktx >= nkt - 8
                            mb = ktx - (nkt - 8)
                            c0 = 128 * (mb // 2) if band else 0

                            def pv(e, ktx=ktx, pr=pr, c0=c0, nkt=nkt,
                                   t=scpB[ktx]):
                                e.wait_ge(sSC.h, t)
                                e.matmul(p_O[0:65, c0:512],
                                         vp[pr % 2][:, ktx, 0, 0:65],
                                         pt[ktx % 6][:, c0:512],
                                         start=(ktx == 0),
                                         stop=(ktx == nkt - 1))
                                e.matmul(p_O[0:65, 512 + c0:1024],
                                         vp[pr % 2][:, ktx, 1, 0:65],
                                         pt[ktx % 6][:, 512 + c0:1024],
                                         start=(ktx == 0),
                                         stop=(ktx == nkt - 1)).then_inc(
                                    sMM.h, 1)
                            eb.on("tensor", pv)
                            pvm.append(sMM.inc())
                            att["tm"] += 0.52

                        for kt in range(nkt):
                            band = kt >= nkt - 8
                            mb = kt - (nkt - 8)
                            c0 = 128 * (mb // 2) if band else 0
                            trit = c_tri_e if (mb % 2 == 0) else c_tri_o

                            def smm(e, j=j, kt=kt, c0=c0, band=band, trit=trit,
                                    khb=khb, pr=pr, qb=qb, qcol=qcol,
                                    tk=(khload[pr] if (kt == 0 and qb == 0)
                                        else None),
                                    tdb=(div_ve[j - 1]
                                         if (kt == 4 and j >= 1) else None),
                                    tspA=(scpA[kt - 2] if kt >= 2 else
                                          (scpj[j - 1][-1] if j >= 1 else None)),
                                    tspB=(scpB[kt - 2] if kt >= 2 else None)):
                                if tk is not None:
                                    e.wait_ge(skh[pr % 2].h, tk)
                                if tdb is not None:
                                    e.wait_ge(sVE.h, tdb[0])
                                if tspA is not None:
                                    e.wait_ge(sSC.h, tspA)
                                sp = p_sp[kt % 2]
                                lhsA = khb[0:64, 128 * kt:128 * (kt + 1)]
                                lhsB = khb[64:128, 128 * kt:128 * (kt + 1)]
                                ins = e.matmul(sp[:, c0:512], lhsA,
                                               qh[pr % 2][0:64,
                                                          qcol + c0:qcol + 512],
                                               start=True, stop=not band)
                                if band:
                                    ins = e.matmul(sp[:, c0:c0 + 128],
                                                   c_ident[:, :], trit[:, :],
                                                   start=False, stop=True)
                                ins.then_inc(sMM.h, 1)
                                if tdb is not None:
                                    e.wait_ge(sVE.h, tdb[1])
                                if tspB is not None:
                                    e.wait_ge(sSC.h, tspB)
                                ins = e.matmul(sp[:, 512 + c0:1024], lhsB,
                                               qh[pr % 2][64:128,
                                                          qcol + c0:qcol + 512],
                                               start=True, stop=not band)
                                if band:
                                    ins = e.matmul(sp[:, 512 + c0:512 + c0 + 128],
                                                   c_ident[:, :], trit[:, :],
                                                   start=False, stop=True)
                                ins.then_inc(sMM.h, 1)
                            eb.on("tensor", smm)
                            smA = sMM.inc()
                            smB = sMM.inc()
                            smark.append(smB)

                            if kt == 3 and j >= 1:
                                emit_division(j - 1, scpB[2])
                            if kt >= 4:
                                emit_pv(kt - 4)

                            def expv(e, j=j, kt=kt, c0=c0, tA=smA, tB=smB,
                                     tpt=(pvm[kt - 6] if kt >= 6 else
                                          (pvj[j - 1][-1] if j >= 1 else None))):
                                e.wait_ge(sMM.h, tA)
                                if tpt is not None:
                                    e.wait_ge(sMM.h, tpt)
                                e.activation(pt[kt % 6][:, c0:512],
                                             p_sp[kt % 2][:, c0:512], AF.Exp,
                                             scale=QSCALE).then_inc(sSC.h, 1)
                                e.wait_ge(sMM.h, tB)
                                e.activation(pt[kt % 6][:, 512 + c0:1024],
                                             p_sp[kt % 2][:, 512 + c0:1024],
                                             AF.Exp,
                                             scale=QSCALE).then_inc(sSC.h, 1)
                            eb.on("scalar", expv)
                            scpA.append(sSC.inc())
                            scpB.append(sSC.inc())
                            att["tm"] += 0.5 + (0.2 if band else 0.0)
                            maybe_ffn()
                        for ktt in range(max(0, nkt - 4), nkt):
                            emit_pv(ktt)

                        def den(e, j=j, t=pvm[-1],
                                td=(dbmark[j - 1] if j >= 1 else None)):
                            e.wait_ge(sMM.h, t)
                            if td is not None:
                                e.wait_ge(sMM.h, td)
                            e.activation(dinv[64:65, :], p_O[64:65, :], AF.Ln)
                            e.activation(dinv[64:65, :], dinv[64:65, :], AF.Exp,
                                         scale=-1.0).then_inc(sSC.h, 1)
                        eb.on("scalar", den)
                        scd.append(sSC.inc())
                        maybe_ffn()

                    emit_division(4 * NPAIR - 1, None)
                    ffn_step(1e12)
                    while pending_cvs:
                        eb.on("vector", pending_cvs.pop(0))
                        popped[0] += 1
                    eb.flush(nc, "pM")

            with contextlib.ExitStack() as s3:
                otsb = s3.enter_context(nc.sbuf_tensor("otsb", [128, DC, SQ], BF16))
                wob = s3.enter_context(nc.sbuf_tensor("wob", [128, DC, 1024], BF16))
                wstg = [s3.enter_context(nc.sbuf_tensor(f"wstg3{i}", [128, 512], F32))
                        for i in range(2)]

                with contextlib.ExitStack() as ps3:
                    p_yp = [ps3.enter_context(nc.psum_tensor(f"p_yp{i}",
                                                             [128, 1024], F32))
                            for i in range(2)]

                    eb = EngBufs()
                    rot3 = {"i": 0, "marks": []}
                    stream_weights(
                        eb,
                        [(wo_T[128 * (i // 2):128 * (i // 2 + 1),
                               512 * (i % 2):512 * (i % 2 + 1)],
                          wob[:, i // 2, 512 * (i % 2):512 * (i % 2 + 1)], 512,
                          None, None) for i in range(2 * DC)],
                        wstg, rot3, swst)
                    t_wo = sSC.n

                    def ld_ot(e, t0=sspo[0].n, t1=sspo[1].n):
                        e.wait_ge(sspo[0].h, t0)
                        e.wait_ge(sspo[1].h, t1)
                        e.dma_start(out=otsb[:, :, :],
                                    in_=outT_scr[:, :, :]).then_inc(sOT.h, 16)
                    eb.on("sync", ld_ot)
                    t_ot = sOT.inc(16)

                    ve_acc = []
                    for st in range(16):
                        def ymm(e, st=st, t_ot=t_ot, t_wo=t_wo,
                                tpv=(ve_acc[st - 2] if st >= 2 else None)):
                            if st == 0:
                                e.wait_ge(sOT.h, t_ot)
                                e.wait_ge(sSC.h, t_wo)
                            if tpv is not None:
                                e.wait_ge(sVE.h, tpv)
                            for ecc in range(DC):
                                for nb in range(2):
                                    ins = e.matmul(
                                        p_yp[st % 2][:, 512 * nb:512 * (nb + 1)],
                                        otsb[:, ecc, 128 * st:128 * (st + 1)],
                                        wob[:, ecc, 512 * nb:512 * (nb + 1)],
                                        start=(ecc == 0), stop=(ecc == DC - 1))
                            ins.then_inc(sMM.h, 1)
                        eb.on("tensor", ymm)
                        mmy = sMM.inc()

                        def yev(e, st=st, t=mmy):
                            e.wait_ge(sMM.h, t)
                            e.tensor_tensor(acc[:, st, :], acc[:, st, :],
                                            p_yp[st % 2][:, :],
                                            ALU.add).then_inc(sVE.h, 1)
                        eb.on("vector", yev)
                        ve_acc.append(sVE.inc())

                        def outdma(e, st=st, t=ve_acc[-1]):
                            e.wait_ge(sVE.h, t)
                            e.dma_start(out=out[128 * st:128 * (st + 1), :],
                                        in_=acc[:, st, :]).then_inc(sOUT.h, 16)
                        eb.on("gpsimd", outdma)
                        sOUT.inc(16)

                    def findrain(e, t=sOUT.n):
                        e.wait_ge(sOUT.h, t)
                    eb.on("gpsimd", findrain)
                    eb.flush(nc, "p3")

    return nc


_NC_CACHE = None


def kernel(L0_attn_x, L0_attn_wq, L0_attn_wk, L0_attn_wv, L0_attn_wo,
           L0_ffn_w_gate, L0_ffn_w_up, L0_ffn_w_down):
    x = np.asarray(L0_attn_x, dtype=np.float32)
    wq = np.asarray(L0_attn_wq, dtype=np.float32)
    wk = np.asarray(L0_attn_wk, dtype=np.float32)
    wv = np.asarray(L0_attn_wv, dtype=np.float32)
    wo = np.asarray(L0_attn_wo, dtype=np.float32)
    wg = np.asarray(L0_ffn_w_gate, dtype=np.int32)
    wu = np.asarray(L0_ffn_w_up, dtype=np.int32)
    wd = np.asarray(L0_ffn_w_down, dtype=np.int32)
    in_maps, own = make_in_maps(x, wq, wk, wv, wo, wg, wu, wd)
    return kernel_run(in_maps, own)


def make_in_maps(x, wq, wk, wv, wo, wg, wu, wd):
    bf = ml_dtypes.bfloat16
    shared = {
        "wq_T": np.ascontiguousarray(wq.T),
        "wk_T": np.ascontiguousarray(wk.T),
        "wv_T": np.ascontiguousarray(wv.T),
        "wo_T": np.ascontiguousarray(wo.T),
        "wg_T": np.ascontiguousarray(wg.T),
        "wu_T": np.ascontiguousarray(wu.T),
        "wd_T": np.ascontiguousarray(wd.T),
        "ident": np.eye(128, dtype=bf),
        "ones_col": np.ones((128, 1), dtype=bf),
        "ones_row_f": np.ones((1, 128), dtype=np.float32),
        "negones_row_f": -np.ones((1, 128), dtype=np.float32),
        "ones_row_b": np.ones((65, 128), dtype=bf),
    }
    pp, cc = np.meshgrid(np.arange(128), np.arange(128), indexing="ij")
    TRI = np.where(cc >= pp, 0.0, NEG).astype(bf)
    ZERO = np.zeros((128, 128), dtype=bf)
    BLOCK = np.full((128, 128), NEG, dtype=bf)

    in_maps = []
    own_tiles_all = []
    for core in range(8):
        b, h = core // 2, core % 2
        own = [2 * i + h for i in range(NQT)]
        own_tiles_all.append(own)
        xb = x[b]
        x_own_a = np.ascontiguousarray(xb.reshape(32, 128, D)[own].reshape(SQ, D))
        m = dict(shared)
        m["x_own"] = x_own_a
        m["xq_T"] = np.ascontiguousarray(x_own_a.T)
        m["xkv_T"] = np.ascontiguousarray(xb.T)
        m["tri_e"] = TRI if h == 0 else ZERO
        m["tri_o"] = BLOCK if h == 0 else TRI
        in_maps.append(m)
    return in_maps, own_tiles_all


def kernel_run(in_maps, own_tiles_all):
    global _NC_CACHE
    if _NC_CACHE is None:
        _NC_CACHE = build_nc()
    res = run_bass_kernel_spmd(_NC_CACHE, in_maps, core_ids=list(range(8)))

    full = np.empty((B, S, D), dtype=np.float32)
    for core in range(8):
        b = core // 2
        oc = np.asarray(res.results[core]["out"]).reshape(NQT, 128, D)
        full[b].reshape(32, 128, D)[own_tiles_all[core]] = oc
    return full
